# version 02
# baseline (speedup 1.0000x reference)
import contextlib
import numpy as np
import ml_dtypes
import concourse.bass as bass
import concourse.mybir as mybir
from concourse.bass_utils import run_bass_kernel_spmd

F32 = mybir.dt.float32
BF16 = mybir.dt.bfloat16
AF = mybir.ActivationFunctionType
ALU = mybir.AluOpType
AX = mybir.AxisListType
BF = ml_dtypes.bfloat16

DIM = 4096
H = 128
PD = 64
G = 8
NS = 128
CS = 128
KCONV = 4
INTER = 8192
CONVD = 10240
NPROJ = 18560
T = 2048
NKC = DIM // 128
NFO = NPROJ // 128
NCH = T // CS
XCH = INTER // 128
EPS = 1e-5
N_HBC = CONVD // 128

ENGINES = ("tensor", "vector", "scalar", "gpsimd", "sync")
FO_KINDS = [("hbc", i) for i in range(N_HBC)] + [("dt", 0)] + [
    ("gate", i) for i in range(INTER // 128)]

ROW_W = 2064
ROW_R = 16



def _host_prep(inputs):
    x = np.asarray(inputs["L0_attn_hidden_states"], dtype=np.float32)
    w_in = np.asarray(inputs["L0_attn_in_proj_w"], dtype=np.float32)
    conv_w = np.asarray(inputs["L0_attn_conv_w"], dtype=np.float32).reshape(CONVD, KCONV)
    conv_b = np.asarray(inputs["L0_attn_conv_b"], dtype=np.float32)
    dt_bias = np.asarray(inputs["L0_attn_dt_bias"], dtype=np.float32)
    a_log = np.asarray(inputs["L0_attn_A_log"], dtype=np.float32)
    d_vec = np.asarray(inputs["L0_attn_D"], dtype=np.float32)
    norm_w = np.asarray(inputs["L0_attn_norm_w"], dtype=np.float32)
    w_out = np.asarray(inputs["L0_attn_out_proj_w"], dtype=np.float32)
    w_base = np.asarray(inputs["L0_ffn_base_w"], dtype=np.float32)
    b_base = np.asarray(inputs["L0_ffn_base_b"], dtype=np.float32)
    a_lora = np.asarray(inputs["L0_ffn_A"], dtype=np.float32)
    b_lora = np.asarray(inputs["L0_ffn_B"], dtype=np.float32)
    w_rtr = np.asarray(inputs["L0_ffn_router_w"], dtype=np.float32)

    perm = np.concatenate([
        np.arange(INTER, INTER + CONVD),
        np.arange(INTER + CONVD, NPROJ),
        np.arange(0, INTER),
    ])
    wp = w_in[perm]
    w_in_b = np.ascontiguousarray(
        wp.reshape(NFO, 128, NKC, 128).transpose(0, 3, 2, 1)).reshape(NFO, 128, DIM)
    w_out_b = np.ascontiguousarray(
        w_out.reshape(8, 512, 64, 128).transpose(0, 2, 3, 1))
    w_base_b = np.ascontiguousarray(
        w_base.reshape(8, 512, 32, 128).transpose(0, 2, 3, 1))
    a_t = np.ascontiguousarray(a_lora.T.reshape(NKC, 128, 64))
    r_t = np.ascontiguousarray(w_rtr.T.reshape(NKC, 128, 16))
    bl = b_lora.reshape(8, 2, 4096, 64)
    b_pair = np.ascontiguousarray(
        bl.transpose(0, 1, 3, 2).reshape(8, 128, 8, 512).transpose(0, 2, 1, 3))

    conv_wb = np.ascontiguousarray(
        conv_w.reshape(N_HBC, 128, KCONV).transpose(1, 0, 2)).reshape(128, N_HBC * KCONV)
    conv_bb = np.ascontiguousarray(conv_b.reshape(N_HBC, 128).T)
    norm_wb = np.ascontiguousarray(norm_w.reshape(XCH, 128).T)
    d_row = np.ascontiguousarray(np.repeat(d_vec, PD)[None, :])

    ml = np.where(np.arange(128)[None, :] < np.arange(128)[:, None], -1e30, 0.0)
    shared = dict(
        w_in_b=w_in_b, w_out_b=w_out_b, w_base_b=w_base_b, a_t=a_t, r_t=r_t,
        b_pair=b_pair, conv_wb=conv_wb, conv_bb=conv_bb, norm_wb=norm_wb,
        d_row=d_row, dt_bias=np.ascontiguousarray(dt_bias[:, None]),
        a_log=np.ascontiguousarray(a_log[:, None]),
        b_base=np.ascontiguousarray(b_base[None, :]),
        ident_bf=np.eye(128, dtype=BF),
        ident_f=np.eye(128, dtype=np.float32),
        nident_f=(-np.eye(128)).astype(np.float32),
        ones_f=np.ones((1, 128), np.float32),
        ones_bf=np.ones((1, 128), BF),
        ml512=np.tile(ml, (1, 4)).astype(BF),
        misc_col=np.tile(np.array([[EPS, 1.0]], np.float32), (128, 1)),
        d_colr=np.ascontiguousarray(d_vec[None, :]),
    )
    in_maps = []
    for core in range(8):
        b, hh = divmod(core, 2)
        m = dict(shared)
        m["x_shard"] = np.ascontiguousarray(x[b, hh * T:(hh + 1) * T])
        m["x_halo"] = (np.zeros((3, DIM), np.float32) if hh == 0
                       else np.ascontiguousarray(x[b, T - 3:T]))
        m["flag"] = np.full((128, 1), float(hh), np.float32)
        in_maps.append(m)
    return in_maps



class MK:
    def __init__(self, debug=False, split=None):
        self.debug = debug
        self.split = split
        self.nc = bass.Bass("TRN2", target_bir_lowering=False)
        self.stack = contextlib.ExitStack()
        self.L = {e: [] for e in ENGINES}
        self._sems = {}

    def sem(self, name):
        if name not in self._sems:
            h = self.stack.enter_context(self.nc.semaphore(name))
            self._sems[name] = [h, 0]
        return self._sems[name]

    def val(self, name):
        return self.sem(name)[1]

    def wait(self, eng, name, value):
        h = self.sem(name)[0]
        self.L[eng].append(lambda e, h=h, v=value: e.wait_ge(h, v))

    def op(self, eng, f, incs=(), waits=()):
        for name, v in waits:
            self.wait(eng, name, v)
        bound = []
        for name, n in incs:
            s = self.sem(name)
            s[1] += n
            bound.append((s[0], n))

        def _f(e, f=f, bound=tuple(bound)):
            ins = f(e)
            for h, n in bound:
                ins = ins.then_inc(h, n)
        self.L[eng].append(_f)

    def bounce(self, eng, via="scalar"):
        n = "bn_" + eng
        tot = self.val(n)
        s = self.sem(n)
        nf = "bnf_" + eng
        sf = self.sem(nf)
        sf[1] += 1
        totf = sf[1]
        def _fwd(e, h=s[0], v=tot, hf=sf[0]):
            e.wait_ge(h, v)
            e.nop().then_inc(hf, 1)
        self.L[via].append(_fwd)
        self.wait(eng, nf, totf)

    def barrier(self, tag):
        name = "bar_" + tag
        s = self.sem(name)
        s[1] += len(ENGINES)
        tot = s[1]
        for eng in ENGINES:
            def _f(e, h=s[0], v=tot):
                e.nop().then_inc(h, 1)
                e.wait_ge(h, v)
            self.L[eng].append(_f)

    def dram(self, name, shape, dtype, kind="Internal"):
        return self.nc.dram_tensor(name, list(shape), dtype, kind=kind).ap()

    def sb(self, stk, name, shape, dtype):
        return stk.enter_context(self.nc.sbuf_tensor(name, list(shape), dtype))

    def psum(self, stk, name, shape, dtype=F32):
        return stk.enter_context(self.nc.psum_tensor(name, list(shape), dtype))

    def build(self):
        nc, dbg = self.nc, ("ExternalOutput" if self.debug else "Internal")
        if self.split == 1:
            mid = "ExternalOutput"
        elif self.split == 2:
            mid = "ExternalInput"
        else:
            mid = dbg
        d = self.dram
        self.x_in = d("x_shard", (T, DIM), F32, "ExternalInput")
        self.x_halo = d("x_halo", (3, DIM), F32, "ExternalInput")
        self.flag_in = d("flag", (128, 1), F32, "ExternalInput")
        self.w_in_b = d("w_in_b", (NFO, 128, DIM), F32, "ExternalInput")
        self.w_out_b = d("w_out_b", (8, 64, 128, 512), F32, "ExternalInput")
        self.w_base_b = d("w_base_b", (8, 32, 128, 512), F32, "ExternalInput")
        self.a_t = d("a_t", (NKC, 128, 64), F32, "ExternalInput")
        self.r_t = d("r_t", (NKC, 128, 16), F32, "ExternalInput")
        self.b_pair = d("b_pair", (8, 8, 128, 512), F32, "ExternalInput")
        self.conv_wb = d("conv_wb", (128, N_HBC * KCONV), F32, "ExternalInput")
        self.conv_bb = d("conv_bb", (128, N_HBC), F32, "ExternalInput")
        self.norm_wb = d("norm_wb", (128, XCH), F32, "ExternalInput")
        self.d_row = d("d_row", (1, INTER), F32, "ExternalInput")
        self.dt_bias = d("dt_bias", (128, 1), F32, "ExternalInput")
        self.a_log = d("a_log", (128, 1), F32, "ExternalInput")
        self.b_base_r = d("b_base", (1, DIM), F32, "ExternalInput")
        self.i_ident_bf = d("ident_bf", (128, 128), BF16, "ExternalInput")
        self.i_ident_f = d("ident_f", (128, 128), F32, "ExternalInput")
        self.i_nident_f = d("nident_f", (128, 128), F32, "ExternalInput")
        self.i_ones_f = d("ones_f", (1, 128), F32, "ExternalInput")
        self.i_ones_bf = d("ones_bf", (1, 128), BF16, "ExternalInput")
        self.i_ml512 = d("ml512", (128, 512), BF16, "ExternalInput")
        self.misc_col = d("misc_col", (128, 2), F32, "ExternalInput")
        self.out_dr = d("out", (T, DIM), F32, "ExternalOutput")
        def dd_mid(name, shape, dtype, _k):
            return d(name, shape, dtype, mid)
        self.xn_dr = d("xn_dr", (T, DIM), BF16, dbg)
        self.xch_dr = dd_mid("xch_dr", (INTER, T), BF16, dbg)
        self.bch_dr = dd_mid("bch_dr", (G * NS, T), BF16, dbg)
        self.cch_dr = dd_mid("cch_dr", (G * NS, T), BF16, dbg)
        self.gate_dr = dd_mid("gate_dr", (INTER, T), BF16, dbg)
        self.dtv_dr = d("dtv_dr", (128, T), F32, dbg)
        self.cum_dr = dd_mid("cum_dr", (NCH, 128, 128), F32, dbg)
        self.dtk_dr = dd_mid("dtk_dr", (128, T), BF16, dbg)
        self.sc2_dr = dd_mid("sc2_dr", (128, T), BF16, dbg)
        self.sc2p_dr = d("sc2p_dr", (128, T), BF16, dbg)
        self.ncum_dr = dd_mid("ncum_dr", (128, T), F32, dbg)
        self.sfin_dr = d(
            "sfin_dr", (128, INTER), F32,
            "ExternalOutput" if (self.split == 1 or self.debug) else "Internal")
        if self.split == 2:
            self.s_peer = d("s_peer", (128, INTER), F32, "ExternalInput")
        self.ccin_dr = d("ccin_dr", (128, INTER), F32, "Internal")
        self.ccout_dr = d("ccout_dr", (256, INTER), F32, "Internal")
        self.d_colr = d("d_colr", (1, 128), F32, "ExternalInput")
        self.yg_dr = d("yg_dr", (T, INTER), BF16, dbg)
        self.x2_dr = d("x2_dr", (T, DIM), F32, dbg)
        self.xn2_dr = d("xn2_dr", (T, DIM), BF16, dbg)
        self.rhe_dr = d("rhe_dr", (NCH, 128, 1024), BF16, "Internal")
        if self.debug:
            self.edbg_dr = d("edbg_dr", (2, 128, 256), F32, "ExternalOutput")

        stk = self.stack
        self.c_ident_bf = self.sb(stk, "c_ident_bf", (128, 128), BF16)
        self.c_ident_f = self.sb(stk, "c_ident_f", (128, 128), F32)
        self.c_nident_f = self.sb(stk, "c_nident_f", (128, 128), F32)
        self.c_ones_f = self.sb(stk, "c_ones_f", (1, 128), F32)
        self.c_ones_bf = self.sb(stk, "c_ones_bf", (1, 128), BF16)
        self.c_ml512 = self.sb(stk, "c_ml512", (128, 512), BF16)
        self.c_convw = self.sb(stk, "c_convw", (128, N_HBC * KCONV), F32)
        self.c_convb = self.sb(stk, "c_convb", (128, N_HBC), F32)
        self.c_normw = self.sb(stk, "c_normw", (128, XCH), F32)
        self.c_dtb = self.sb(stk, "c_dtb", (128, 1), F32)
        self.c_alog = self.sb(stk, "c_alog", (128, 1), F32)
        self.c_flag = self.sb(stk, "c_flag", (128, 1), F32)
        self.c_misc = self.sb(stk, "c_misc", (128, 2), F32)

        for t_, s_ in [
            (self.c_ident_bf, self.i_ident_bf), (self.c_ident_f, self.i_ident_f),
            (self.c_nident_f, self.i_nident_f), (self.c_ones_f, self.i_ones_f),
            (self.c_ones_bf, self.i_ones_bf), (self.c_ml512, self.i_ml512),
            (self.c_convw, self.conv_wb), (self.c_convb, self.conv_bb),
            (self.c_normw, self.norm_wb), (self.c_dtb, self.dt_bias),
            (self.c_alog, self.a_log), (self.c_flag, self.flag_in),
            (self.c_misc, self.misc_col),
        ]:
            self.op("gpsimd", lambda e, t=t_, s=s_: e.dma_start(out=t[:, :], in_=s),
                    incs=[("c_ld", 16)])
        self.wait("gpsimd", "c_ld", self.val("c_ld"))
        self.barrier("consts")

        if self.split != 2:
            with contextlib.ExitStack() as stAB:
                self.xn_fm = self.sb(stAB, "xn_fm", (128, NKC * T), BF16)
                self.xn_hl = self.sb(stAB, "xn_hl", (128, NKC * 4), BF16)
                self.phase_A(stAB)
                self.phase_B(stAB)
            self.barrier("B")

        with contextlib.ExitStack() as stC:
            self.phase_C(stC)
        self.barrier("C")

        if self.split != 1:
            with contextlib.ExitStack() as stD:
                self.phase_D(stD)
            self.barrier("D")
            with contextlib.ExitStack() as stE:
                self.phase_E(stE)

        self.install()
        return nc

    def phase_A(self, stAB):
        with contextlib.ExitStack() as stk:
            xa = [self.sb(stk, f"xa{i}", (128, DIM), F32) for i in range(2)]
            xat = [self.sb(stk, f"xat{i}", (128, DIM), BF16) for i in range(2)]
            a_scr = self.sb(stk, "a_scr", (128, DIM), F32)
            a_col = self.sb(stk, "a_col", (128, 8), F32)
            xat_halo = self.sb(stk, "xat_halo", (3, DIM), BF16)
            ps_hb = self.psum(stk, "ps_ahalob", (128, 8), BF16)

            def ln_chunk(i, n_tok, src_ap, dst_tile, spill_dst):
                buf = xa[i % 2]
                cb = (i % 2) * 4
                self.op("sync", lambda e, b=buf, s=src_ap, n=n_tok: e.dma_start(
                    out=b[0:n, :], in_=s), incs=[("a_ld", 16)],
                    waits=[("a_xn", max(0, self.val("a_xn") - 1))])
                self.op("scalar", lambda e, b=buf, n=n_tok, cb=cb: e.activation(
                    a_scr[0:n, :], b[0:n, :], AF.Copy,
                    accum_out=a_col[0:n, cb:cb + 1]),
                    incs=[("a_s1", 1)],
                    waits=[("a_ld", self.val("a_ld")),
                           ("a_xn", max(0, self.val("a_xn") - 1))])
                self.op("vector", lambda e, n=n_tok, cb=cb: e.tensor_scalar(
                    a_col[0:n, cb:cb + 1], a_col[0:n, cb:cb + 1], -1.0 / DIM,
                    None, ALU.mult),
                    incs=[("a_nm", 1)], waits=[("a_s1", self.val("a_s1"))])
                self.op("scalar", lambda e, b=buf, n=n_tok, cb=cb: e.activation(
                    a_scr[0:n, :], b[0:n, :], AF.Square,
                    bias=a_col[0:n, cb:cb + 1], scale=1.0,
                    accum_out=a_col[0:n, cb + 1:cb + 2]),
                    waits=[("a_nm", self.val("a_nm"))])
                self.op("scalar", lambda e, n=n_tok, cb=cb: e.activation(
                    a_col[0:n, cb + 2:cb + 3], a_col[0:n, cb + 1:cb + 2], AF.Ln,
                    bias=self.c_misc[0:n, 0:1], scale=1.0 / DIM))
                self.op("scalar", lambda e, n=n_tok, cb=cb: e.activation(
                    a_col[0:n, cb + 3:cb + 4], a_col[0:n, cb + 2:cb + 3], AF.Exp,
                    scale=-0.5),
                    incs=[("a_rs", 1)])
                w = [("a_rs", self.val("a_rs"))]
                if i >= 2:
                    w.append(("a_sp", 16 * max(0, self.val("a_sp") // 16 - 1)))
                self.op("vector", lambda e, b=buf, bt=dst_tile, n=n_tok, cb=cb:
                        e.tensor_scalar(
                            bt[0:n, :], b[0:n, :], a_col[0:n, cb:cb + 1],
                            a_col[0:n, cb + 3:cb + 4], ALU.add, ALU.mult),
                        incs=[("a_xn", 1)], waits=w)
                if spill_dst is not None:
                    self.op("sync", lambda e, bt=dst_tile, ds=spill_dst, n=n_tok:
                            e.dma_start(out=ds, in_=bt[0:n, :]),
                            incs=[("a_sp", 16)],
                            waits=[("a_xn", self.val("a_xn"))])

            for i in range(NCH):
                ln_chunk(i, 128, self.x_in[i * 128:(i + 1) * 128, :], xat[i % 2],
                         self.xn_dr[i * 128:(i + 1) * 128, :])
            ln_chunk(NCH, 3, self.x_halo, xat_halo, None)

            self.wait("sync", "a_sp", self.val("a_sp"))
            for kc in range(NKC):
                self.op("sync", lambda e, kc=kc: e.dma_start_transpose(
                    self.xn_fm[:, kc * T: (kc + 1) * T],
                    self.xn_dr[:, kc * 128:(kc + 1) * 128]),
                    incs=[("a_tr", 16)])
            self.wait("tensor", "a_xn", self.val("a_xn"))
            for kc in range(NKC):
                w = [] if kc == 0 else [("a_hcp", self.val("a_hcp"))]
                self.op("tensor", lambda e, kc=kc: e.transpose(
                    ps_hb[:, 0:3], xat_halo[0:3, kc * 128:(kc + 1) * 128],
                    self.c_ident_bf[0:3, 0:3]),
                    incs=[("a_pet", 1)], waits=w)
                self.op("vector", lambda e, kc=kc: e.tensor_copy(
                    self.xn_hl[:, kc * 4: kc * 4 + 3],
                    ps_hb[:, 0:3]),
                    incs=[("a_hcp", 1)], waits=[("a_pet", self.val("a_pet"))])
            self.wait("sync", "a_tr", self.val("a_tr"))
            self.barrier("A")

    def phase_B(self, stAB):
        with contextlib.ExitStack() as stk:
            wsb = [self.sb(stk, f"wsb{i}", (128, DIM), BF16) for i in range(2)]
            ps_main = [self.psum(stk, f"ps_b{i}", (128, 512)) for i in range(2)]
            ps_halo = self.psum(stk, "ps_bh", (128, 8))
            row = [self.sb(stk, f"row{i}", (128, ROW_W), F32) for i in range(2)]
            crow = [self.sb(stk, f"crow{i}", (128, T), BF16) for i in range(2)]
            cv_a = self.sb(stk, "cv_a", (128, T), F32)
            cv_b = self.sb(stk, "cv_b", (128, T), F32)
            orow = [self.sb(stk, f"orow{i}", (128, T), BF16) for i in range(2)]
            self.dt_raw = self.sb(stk, "dt_raw", (128, T), F32)

            def fo_dst(kind, idx):
                if kind == "hbc":
                    if idx < XCH:
                        return self.xch_dr[idx * 128:(idx + 1) * 128, :]
                    if idx < XCH + 8:
                        j = idx - XCH
                        return self.bch_dr[j * 128:(j + 1) * 128, :]
                    j = idx - XCH - 8
                    return self.cch_dr[j * 128:(j + 1) * 128, :]
                return self.gate_dr[idx * 128:(idx + 1) * 128, :]

            n_spill = 0
            n_hbc_seen = 0
            for fi, (kind, idx) in enumerate(FO_KINDS):
                w = []
                if fi >= 2:
                    k2 = FO_KINDS[fi - 2][0]
                    if k2 == "hbc":
                        w = [("b_hev", fi - 1)]
                    else:
                        w = [("b_ev", 4 * (fi - 1))]
                self.op("gpsimd", lambda e, fi=fi: e.dma_start(
                    out=wsb[fi % 2][:, :], in_=self.w_in_b[fi]),
                    incs=[("b_wd", 16)], waits=w)

                wt = wsb[fi % 2]
                self.wait("tensor", "b_wd", 16 * (fi + 1))
                n_tiles = 5 if kind == "hbc" else 4
                for tt in range(4):
                    g = fi * 4 + tt
                    if g >= 2:
                        self.wait("tensor", "b_ev", g - 1)
                    bank = ps_main[tt % 2]
                    for kc in range(NKC):
                        last = (kc == NKC - 1)
                        incs = [("b_mmt", 1)] if last else []
                        self.op("tensor", lambda e, wt=wt, kc=kc, bank=bank, tt=tt,
                                st=(kc == 0), sp=(kc == NKC - 1): e.matmul(
                                    bank[:, :], wt[:, kc * 128:(kc + 1) * 128],
                                    self.xn_fm[:, kc * T + tt * 512:
                                               kc * T + (tt + 1) * 512],
                                    start=st, stop=sp), incs=incs)
                if kind == "hbc":
                    if self.val("b_hev") > 0:
                        self.wait("tensor", "b_hev", self.val("b_hev"))
                    for kc in range(NKC):
                        last = (kc == NKC - 1)
                        incs = ([("b_hmm", 1)] if last else [])
                        self.op("tensor", lambda e, wt=wt, kc=kc,
                                st=(kc == 0), sp=(kc == NKC - 1): e.matmul(
                                    ps_halo[:, 0:3],
                                    wt[:, kc * 128:(kc + 1) * 128],
                                    self.xn_hl[:, kc * 4: kc * 4 + 3],
                                    start=st, stop=sp), incs=incs)

                r = row[fi % 2]
                if kind == "hbc":
                    hb_i = idx
                    if fi >= 2:
                        self.wait("scalar", "b_cv", max(0, self.val("b_cv") - 1))
                    for tt in range(4):
                        self.op("scalar", lambda e, r=r, tt=tt, bank=ps_main[tt % 2]:
                                e.activation(
                                    r[:, ROW_R + tt * 512:ROW_R + (tt + 1) * 512],
                                    bank[:, :], AF.Copy),
                                incs=[("b_ev", 1)],
                                waits=[("b_mmt", fi * 4 + tt + 1)])
                    self.op("scalar", lambda e, r=r: e.activation(
                        r[:, 13:16], ps_halo[:, 0:3], AF.Copy),
                        incs=[("b_hev", 1)], waits=[("b_hmm", self.val("b_hmm"))])
                    cw = [self.c_convw[:, hb_i * KCONV + k: hb_i * KCONV + k + 1]
                          for k in range(KCONV)]
                    wv = [("b_ev", self.val("b_ev")), ("b_hev", self.val("b_hev"))]
                    if self.val("b_sl") > 0:
                        wv.append(("b_sl", self.val("b_sl")))
                    self.op("vector", lambda e, r=r, c0=cw[0]: e.tensor_scalar(
                        cv_a[:, :], r[:, 13:13 + T], c0, None, ALU.mult), waits=wv)
                    self.op("vector", lambda e, r=r, c1=cw[1]:
                            e.scalar_tensor_tensor(cv_b[:, :], r[:, 14:14 + T], c1,
                                                   cv_a[:, :], ALU.mult, ALU.add))
                    self.op("vector", lambda e, r=r, c2=cw[2]:
                            e.scalar_tensor_tensor(cv_a[:, :], r[:, 15:15 + T], c2,
                                                   cv_b[:, :], ALU.mult, ALU.add))
                    self.op("vector", lambda e, r=r, c3=cw[3]:
                            e.scalar_tensor_tensor(cv_b[:, :], r[:, 16:16 + T], c3,
                                                   cv_a[:, :], ALU.mult, ALU.add),
                            incs=[("b_cv", 1)])
                    wv2 = [("b_cv", self.val("b_cv"))]
                    if n_spill >= 2:
                        wv2.append(("b_spl", 16 * (n_spill - 1)))
                    self.op("scalar", lambda e, o=orow[fi % 2], hb_i=hb_i:
                            e.activation(o[:, :], cv_b[:, :], AF.Silu,
                                         bias=self.c_convb[:, hb_i:hb_i + 1],
                                         scale=1.0),
                            incs=[("b_sl", 1)], waits=wv2)
                    self.op("sync", lambda e, o=orow[fi % 2], ds=fo_dst(kind, idx):
                            e.dma_start(out=ds, in_=o[:, :]),
                            incs=[("b_spl", 16)], waits=[("b_sl", self.val("b_sl"))])
                    n_spill += 1
                elif kind == "dt":
                    for tt in range(4):
                        self.op("scalar", lambda e, tt=tt, bank=ps_main[tt % 2]:
                                e.activation(self.dt_raw[:, tt * 512:(tt + 1) * 512],
                                             bank[:, :], AF.Copy),
                                incs=[("b_ev", 1)],
                                waits=[("b_mmt", fi * 4 + tt + 1)])
                else:
                    gi = idx
                    wv = []
                    if FO_KINDS[fi - 2][0] == "gate":
                        wv.append(("b_sl", max(0, self.val("b_sl") - 1)))
                    for tt in range(4):
                        self.op("scalar", lambda e, tt=tt, bank=ps_main[tt % 2],
                                o=crow[fi % 2]: e.activation(
                                    o[:, tt * 512:(tt + 1) * 512], bank[:, :],
                                    AF.Silu),
                                incs=[("b_ev", 1)],
                                waits=[("b_mmt", fi * 4 + tt + 1)] + (wv if tt == 0 else []))
                    wv2 = [("b_ev", self.val("b_ev"))]
                    if n_spill >= 2:
                        wv2.append(("b_spl", 16 * (n_spill - 1)))
                    self.op("vector", lambda e, gi=gi, i_=crow[fi % 2], o=orow[fi % 2]:
                            e.tensor_scalar(o[:, :], i_[:, :],
                                            self.c_normw[:, gi:gi + 1], None,
                                            ALU.mult),
                            incs=[("b_sl", 1)], waits=wv2)
                    self.op("sync", lambda e, o=orow[fi % 2], ds=fo_dst(kind, idx):
                            e.dma_start(out=ds, in_=o[:, :]),
                            incs=[("b_spl", 16)], waits=[("b_sl", self.val("b_sl"))])
                    n_spill += 1

                if kind == "dt":
                    self.phase_dt(stk, cv_a, cv_b)

            self.wait("sync", "b_spl", 16 * n_spill)
            self.wait("sync", "dt_sp", self.val("dt_sp"))

    def phase_dt(self, stk, cv_a, cv_b):
        dt_ch, cum_ch = cv_a, cv_b
        d_scr = cum_ch
        d_adt = self.dt_raw
        d_col = self.sb(stk, "d_col", (128, 4), F32)
        d_zero = self.sb(stk, "d_zero", (128, 128), F32)
        d_tok = self.sb(stk, "d_tok", (128, 128), BF16)
        d_tokf = self.sb(stk, "d_tokf", (128, 128), F32)
        d_prod = self.sb(stk, "d_prod", (128, 128), F32)
        ps_t = self.psum(stk, "ps_dt", (128, 128))

        self.wait("vector", "b_sl", N_HBC)
        self.wait("scalar", "b_sl", N_HBC)
        self.wait("scalar", "b_ev", self.val("b_ev"))
        self.op("scalar", lambda e: e.activation(
            d_scr[:, :], self.dt_raw[:, :], AF.Abs, bias=self.c_dtb[:, 0:1]))
        self.op("scalar", lambda e: e.activation(
            d_scr[:, :], d_scr[:, :], AF.Exp, scale=-1.0))
        self.op("scalar", lambda e: e.activation(
            d_scr[:, :], d_scr[:, :], AF.Ln, bias=self.c_misc[:, 1:2]),
            incs=[("dt_a", 1)])
        self.op("vector", lambda e: e.tensor_scalar(
            dt_ch[:, :], self.dt_raw[:, :], self.c_dtb[:, 0:1], 0.0,
            ALU.add, ALU.max), waits=[("dt_a", self.val("dt_a"))])
        self.op("vector", lambda e: e.tensor_tensor(
            dt_ch[:, :], dt_ch[:, :], d_scr[:, :], ALU.add), incs=[("dt_v", 1)])
        self.op("scalar", lambda e: e.activation(
            d_col[:, 0:1], self.c_alog[:, 0:1], AF.Exp), incs=[("dt_a", 1)],
            waits=[("dt_v", self.val("dt_v"))])
        self.op("vector", lambda e: e.tensor_scalar(
            d_adt[:, :], dt_ch[:, :], d_col[:, 0:1], -1.0, ALU.mult, ALU.mult),
            waits=[("dt_a", self.val("dt_a"))])
        self.op("vector", lambda e: e.memset(d_zero[:, :], 0.0),
                incs=[("bn_vector", 1)])
        self.bounce("vector")
        for c in range(NCH):
            self.op("vector", lambda e, c=c: e.tensor_tensor_scan(
                cum_ch[:, c * 128:(c + 1) * 128], d_adt[:, c * 128:(c + 1) * 128],
                d_zero[:, :], 0.0, ALU.add, ALU.add),
                incs=([("bn_vector", 1)] if c == NCH - 1 else []))
        self.bounce("vector")
        d_suf = self.sb(stk, "d_suf", (128, NCH), F32)
        d_ends = self.sb(stk, "d_ends", (128, NCH), F32)
        d_pre = self.sb(stk, "d_pre", (128, NCH), F32)
        self.op("vector", lambda e: e.tensor_copy(
            d_ends[:, :], cum_ch[:, 127:T:128]), incs=[("bn_vector", 1)])
        self.bounce("vector")
        self.op("vector", lambda e: e.tensor_tensor_scan(
            d_pre[:, :], d_ends[:, :], d_zero[:, 0:NCH], 0.0, ALU.add, ALU.add),
            incs=[("bn_vector", 1)])
        self.bounce("vector")
        self.op("vector", lambda e: e.tensor_scalar(
            d_suf[:, :], d_pre[:, :], d_pre[:, NCH - 1:NCH], -1.0,
            ALU.subtract, ALU.mult), incs=[("bn_vector", 1)])
        self.bounce("vector")
        self.op("vector", lambda e: e.tensor_tensor(
            d_suf[:, :], d_suf[:, :], d_ends[:, :], ALU.add),
            incs=[("dt_v", 1)])
        self.op("sync", lambda e: e.dma_start(out=self.dtv_dr, in_=dt_ch[:, :]),
                incs=[("dt_sp", 16)], waits=[("dt_v", self.val("dt_v"))])
        for c in range(NCH):
            self.op("sync", lambda e, c=c: e.dma_start(
                out=self.cum_dr[c], in_=cum_ch[:, c * 128:(c + 1) * 128]),
                incs=[("dt_sp", 16)])

        def tok_spill(c, src_f32_tile, dst, as_f32, extra_waits=()):
            ident = self.c_nident_f if dst is self.ncum_dr else self.c_ident_f
            self.op("tensor", lambda e, s=src_f32_tile, idt=ident: e.transpose(
                ps_t[:, :], s, idt[:, :]),
                incs=[("dt_pe", 1)],
                waits=[("dt_cp", self.val("dt_cp")),
                       ("dt_v", self.val("dt_v"))] + list(extra_waits))
            tgt = d_tokf if as_f32 else d_tok
            neg = (dst is self.ncum_dr)
            self.op("vector", lambda e, t=tgt, neg=neg: (
                e.tensor_scalar_mul(t[:, :], ps_t[:, :], -1.0) if neg
                else e.tensor_copy(t[:, :], ps_t[:, :])),
                incs=[("dt_cp", 1)],
                waits=[("dt_pe", self.val("dt_pe")),
                       ("dt_sp", self.val("dt_sp"))])
            self.op("sync", lambda e, t=tgt, ds=dst, c=c: e.dma_start(
                out=ds[:, c * 128:(c + 1) * 128], in_=t[:, :]),
                incs=[("dt_sp", 16)], waits=[("dt_cp", self.val("dt_cp"))])

        for c in range(NCH):
            tok_spill(c, cum_ch[:, c * 128:(c + 1) * 128], self.ncum_dr, True)
            tok_spill(c, dt_ch[:, c * 128:(c + 1) * 128], self.dtk_dr, False)
            for dst, bias_ap in ((self.sc2_dr, cum_ch[:, c * 128 + 127:c * 128 + 128]),
                                 (self.sc2p_dr, d_suf[:, c:c + 1])):
                self.op("scalar", lambda e, c=c, ba=bias_ap: e.activation(
                    d_prod[:, :], cum_ch[:, c * 128:(c + 1) * 128], AF.Exp,
                    bias=ba, scale=-1.0), incs=[("dt_a", 1)],
                    waits=[("dt_v", self.val("dt_v")),
                           ("dt_pr", self.val("dt_pr")),
                           ("dt_pe", self.val("dt_pe"))])
                self.op("vector", lambda e, c=c: e.tensor_tensor(
                    d_prod[:, :], dt_ch[:, c * 128:(c + 1) * 128], d_prod[:, :],
                    ALU.mult), incs=[("dt_pr", 1)],
                    waits=[("dt_a", self.val("dt_a"))])
                tok_spill(c, d_prod[:, :], dst, False,
                          extra_waits=[("dt_pr", self.val("dt_pr"))])

    def phase_C(self, stC):
        v = self.val
        r_dtk = self.sb(stC, "r_dtk", (128, T), BF16)
        r_sc2 = self.sb(stC, "r_sc2", (128, T), BF16)
        r_ncum = self.sb(stC, "r_ncum", (128, T), F32)
        x_tok = [self.sb(stC, f"x_tok{i}", (128, INTER), BF16) for i in range(2)]
        b_tok = [self.sb(stC, f"b_tok{i}", (128, G * NS), BF16) for i in range(2)]
        s_sb = self.sb(stC, "s_sb", (128, INTER), F32)

        if self.split != 1:
            for t_, s_ in [(r_dtk, self.dtk_dr), (r_sc2, self.sc2_dr),
                           (r_ncum, self.ncum_dr)]:
                self.op("gpsimd", lambda e, t=t_, s=s_: e.dma_start(
                    out=t[:, :], in_=s), incs=[("cl_r", 16)])

        def load_xb(c, waits):
            self.op("sync", lambda e, c=c: e.dma_start_transpose(
                x_tok[c % 2][:, :], self.xch_dr[:, c * 128:(c + 1) * 128]),
                incs=[("cl_x", 16)], waits=waits)
            self.op("sync", lambda e, c=c: e.dma_start_transpose(
                b_tok[c % 2][:, :], self.bch_dr[:, c * 128:(c + 1) * 128]),
                incs=[("cl_x", 16)])

        if self.split == 2:
            with contextlib.ExitStack() as stCC:
                c_gath = self.sb(stCC, "c_gath", (128, INTER), F32)
                self.op("sync", lambda e: e.dma_start(
                    out=c_gath[:, :], in_=self.s_peer), incs=[("cc_d", 16)])
                self.op("vector", lambda e: e.tensor_scalar(
                    s_sb[:, :], c_gath[:, :], self.c_flag[:, 0:1], None,
                    ALU.mult), incs=[("cc_v", 1)],
                    waits=[("cc_d", v("cc_d")), ("cl_r", v("cl_r"))])
                self.wait("tensor", "cc_v", 1)
                self.wait("scalar", "cc_v", 1)
                self.wait("gpsimd", "cc_v", 1)
                self.barrier("CC")
        stC1_ctx = (contextlib.ExitStack() if self.split != 2
                    else contextlib.nullcontext())
        if self.split != 2:
          pass
        with stC1_ctx as stC1:
          if self.split != 2:
            r_sc2p = self.sb(stC1, "r_sc2p", (128, T), BF16)
            xd2c1 = self.sb(stC1, "xd2c1", (128, INTER // 2), BF16)
            ps_c1 = self.psum(stC1, "ps_c1", (128, INTER // 2))
            self.op("gpsimd", lambda e: e.dma_start(
                out=r_sc2p[:, :], in_=self.sc2p_dr), incs=[("cl_r", 16)])
            self.wait("gpsimd", "cl_r", v("cl_r"))
            self.wait("vector", "cl_r", v("cl_r"))

            for it in range(2 * NCH):
                hp, c = divmod(it, NCH)
                if True:
                    if it == 0:
                        load_xb(0, waits=[])
                        load_xb(1, waits=[])
                    self.op("vector", lambda e, c=c, hp=hp: e.tensor_tensor(
                        xd2c1[:, :].rearrange("p (h q) -> p h q", q=PD),
                        x_tok[c % 2][:, hp * 4096:(hp + 1) * 4096].rearrange(
                            "p (h q) -> p h q", q=PD),
                        r_sc2p[:, c * 128 + hp * 64: c * 128 + (hp + 1) * 64]
                        .unsqueeze(2).broadcast_to((128, 64, PD)), ALU.mult),
                        incs=[("c1_v", 1)],
                        waits=[("cl_x", 32 * (it + 1)),
                               ("c1_pe", max(0, v("c1_pe") - 1))])
                    self.wait("tensor", "c1_v", v("c1_v"))
                    if hp == 1 and c == 0:
                        self.wait("tensor", "c1_ev", 8)
                    for g in range(4):
                        gg = hp * 4 + g
                        for hl in range(16):
                            hh = g * 16 + hl
                            st = (c == 0 and hh % 8 == 0)
                            sp = (c == NCH - 1 and hh % 8 == 7)
                            incs = ([("c1_pe", 1)]
                                    if (g == 3 and hl == 15) else [])
                            self.op("tensor", lambda e, c=c, gg=gg, hh=hh,
                                    st=st, sp=sp: e.matmul(
                                        ps_c1[:, hh * PD:(hh + 1) * PD],
                                        b_tok[c % 2][:, gg * 128:(gg + 1) * 128],
                                        xd2c1[:, hh * PD:(hh + 1) * PD],
                                        start=st, stop=sp,
                                        skip_group_check=True), incs=incs)
                    if it + 2 < 2 * NCH:
                        load_xb((it + 2) % NCH,
                                waits=[("c1_pe", v("c1_pe"))])
                    if c == NCH - 1:
                        self.wait("scalar", "c1_pe", v("c1_pe"))
                        for j in range(8):
                            self.op("scalar", lambda e, hp=hp, j=j: e.activation(
                                s_sb[:, hp * 4096 + j * 512:
                                     hp * 4096 + (j + 1) * 512],
                                ps_c1[:, j * 512:(j + 1) * 512], AF.Copy),
                                incs=[("c1_ev", 1)])

            if self.split == 1 or self.debug:
                self.op("sync", lambda e: e.dma_start(
                    out=self.sfin_dr, in_=s_sb[:, :]), incs=[("cc_d", 16)],
                    waits=[("c1_ev", v("c1_ev"))])
                self.wait("sync", "cc_d", v("cc_d"))
            if self.split is None:
                c_gath = self.sb(stC1, "c_gath", (128, INTER), F32)
                self.op("sync", lambda e: e.dma_start(
                    out=self.ccin_dr, in_=s_sb[:, :]), incs=[("cc_d", 16)],
                    waits=[("c1_ev", v("c1_ev"))])
                self.wait("gpsimd", "cc_d", v("cc_d"))
                self.op("gpsimd", lambda e: e.collective_compute(
                    "AllReduce", ALU.add,
                    replica_groups=[[0, 1], [2, 3], [4, 5], [6, 7]],
                    ins=[self.ccin_dr], outs=[self.ccout_dr]),
                    incs=[("cc_s", 16)])
                self.wait("sync", "cc_s", 16)
                self.op("sync", lambda e: e.dma_start(
                    out=c_gath[:, :], in_=self.ccout_dr[0:128, :]),
                    incs=[("cc_d", 16)])
                self.op("vector", lambda e: e.tensor_scalar(
                    s_sb[:, :], c_gath[:, :], self.c_flag[:, 0:1], None,
                    ALU.mult),
                    incs=[("cc_v", 1)], waits=[("cc_d", v("cc_d"))])
                self.wait("tensor", "cc_v", 1)
                self.wait("scalar", "cc_v", 1)
                self.wait("gpsimd", "cc_v", 1)

        if self.split == 1:
            return
        with contextlib.ExitStack() as stC2:
            x_dt = self.sb(stC2, "x_dt", (128, INTER), BF16)
            xd2 = self.sb(stC2, "xd2", (128, INTER), BF16)
            s_bf = self.sb(stC2, "s_bf", (128, INTER), BF16)
            yf = self.sb(stC2, "yf", (128, INTER), BF16)
            gate_t = self.sb(stC2, "gate_t", (128, INTER), BF16)
            b_ch = [self.sb(stC2, f"b_ch{i}", (128, G * NS), BF16) for i in range(2)]
            c_ch = [self.sb(stC2, f"c_ch{i}", (128, G * NS), BF16) for i in range(2)]
            e_all = [self.sb(stC2, f"e_all{i}", (128, 1024), BF16) for i in range(2)]
            mt = [self.sb(stC2, f"mt{i}", (128, 1024), BF16) for i in range(2)]
            g2sb = [self.sb(stC2, f"g2sb{i}", (128, 128), BF16) for i in range(2)]
            sdec = self.sb(stC2, "sdec", (128, 128), F32)
            b_dtot = self.sb(stC2, "b_dtot", (128, 128), F32)
            d_rep = self.sb(stC2, "d_rep", (128, 128), F32)
            xd_blk = [self.sb(stC2, f"xd_blk{i}", (128, 512), BF16) for i in range(3)]
            ta_s = [self.sb(stC2, f"ta_s{i}", (128, 512), BF16) for i in range(2)]
            tb_s = [self.sb(stC2, f"tb_s{i}", (128, 512), BF16) for i in range(2)]
            cfl = [self.sb(stC2, f"cfl{i}", (1, 1024), F32) for i in range(2)]
            rcol = self.sb(stC2, "rcol", (128, 4), F32)
            ps_g2 = self.psum(stC2, "ps_g2", (128, 128))
            ps_bc = self.psum(stC2, "ps_bc", (128, 512))
            ps_yd = [self.psum(stC2, f"ps_yd{i}", (128, 512)) for i in range(2)]
            ps_yo = [self.psum(stC2, f"ps_yo{i}", (128, 512)) for i in range(2)]
            ps_st = [self.psum(stC2, f"ps_st{i}", (128, 512)) for i in range(2)]

            self.op("gpsimd", lambda e: e.dma_start(
                out=d_rep[0:1, 0:128], in_=self.d_colr), incs=[("cl_r", 16)])
            self.wait("tensor", "cl_r", v("cl_r"))
            self.op("tensor", lambda e: e.matmul(
                ps_bc[:, 0:128], self.c_ones_f[0:1, :], d_rep[0:1, 0:128],
                start=True, stop=True), incs=[("c2_t", 1)])
            self.op("scalar", lambda e: e.activation(
                d_rep[:, :], ps_bc[:, 0:128], AF.Copy),
                incs=[("c2_drep", 1)], waits=[("c2_t", v("c2_t"))])

            def emit_tb(k):
                pb = k % 2
                self.op("vector", lambda e, pb=pb: e.tensor_tensor(
                    tb_s[pb][:, :], ta_s[pb][:, :], ps_yd[pb][:, :], ALU.add),
                    incs=[("c2_tb", 1)])

            def emit_yf_stt(k):
                pb = k % 2
                h0 = (k % 16) * 8
                c = k // 16
                self.op("vector", lambda e, pb=pb, h0=h0, k=k: e.tensor_tensor(
                    yf[:, h0 * PD:(h0 + 8) * PD], tb_s[pb][:, :],
                    xd_blk[k % 3][:, :], ALU.add),
                    incs=[("c2_yev", 1)],
                    waits=([("c2_xd", v("c2_xd"))] if True else []) +
                          ([("c2_sq", v("c2_sq"))] if h0 == 0 else []))
                for j in range(8):
                    h = h0 + j
                    self.op("vector", lambda e, j=j, h=h, pb=pb:
                            e.scalar_tensor_tensor(
                                s_sb[:, h * PD:(h + 1) * PD],
                                s_sb[:, h * PD:(h + 1) * PD],
                                b_dtot[:, h:h + 1],
                                ps_st[pb][:, j * PD:(j + 1) * PD],
                                ALU.mult, ALU.add),
                            incs=([("c2_sup", 1)] if j == 7 else []))

            def emit_ta(k):
                pb = k % 2
                h0 = (k % 16) * 8
                self.op("vector", lambda e, pb=pb, h0=h0: e.tensor_tensor(
                    ta_s[pb][:, :].rearrange("p (h q) -> p h q", q=PD),
                    ps_yo[pb][:, :].rearrange("p (h q) -> p h q", q=PD),
                    sdec[:, h0:h0 + 8].unsqueeze(2)
                    .broadcast_to((128, 8, PD)), ALU.mult),
                    incs=[("c2_ta", 1)],
                    waits=[("c2_ymm", v("c2_ymm"))])

            kblk = 0
            for c in range(NCH):
                lw = [("c2_done", max(0, v("c2_done") - 1))]
                self.op("sync", lambda e, c=c: e.dma_start_transpose(
                    x_tok[c % 2][:, :], self.xch_dr[:, c * 128:(c + 1) * 128]),
                    incs=[("cl_x2", 16)], waits=lw)
                self.op("sync", lambda e, c=c: e.dma_start_transpose(
                    b_tok[c % 2][:, :], self.bch_dr[:, c * 128:(c + 1) * 128]),
                    incs=[("cl_x2", 16)])
                self.op("sync", lambda e, c=c: e.dma_start_transpose(
                    gate_t[:, :], self.gate_dr[:, c * 128:(c + 1) * 128]),
                    incs=[("cl_g", 16)],
                    waits=[("c2_yg", max(0, v("c2_yg") - 1))])
                self.op("gpsimd", lambda e, c=c: e.dma_start(
                    out=b_ch[c % 2][:, :],
                    in_=self.bch_dr.rearrange("(g p) t -> p g t", p=128)
                    [:, :, c * 128:(c + 1) * 128]),
                    incs=[("cl_bc", 16)], waits=lw)
                self.op("gpsimd", lambda e, c=c: e.dma_start(
                    out=c_ch[c % 2][:, :],
                    in_=self.cch_dr.rearrange("(g p) t -> p g t", p=128)
                    [:, :, c * 128:(c + 1) * 128]),
                    incs=[("cl_bc", 16)])
                self.op("gpsimd", lambda e, c=c: e.tensor_tensor(
                    x_dt[:, :].rearrange("p (h q) -> p h q", q=PD),
                    x_tok[c % 2][:, :].rearrange("p (h q) -> p h q", q=PD),
                    r_dtk[:, c * 128:(c + 1) * 128].unsqueeze(2)
                    .broadcast_to((128, 128, PD)), ALU.mult),
                    incs=[("c2_p", 1)],
                    waits=[("cl_x2", 32 * (c + 1)), ("cl_r", v("cl_r")),
                           ("c2_done2", max(0, v("c2_done2") - 1))])
                self.op("gpsimd", lambda e, c=c: e.tensor_tensor(
                    xd2[:, :].rearrange("p (h q) -> p h q", q=PD),
                    x_tok[c % 2][:, :].rearrange("p (h q) -> p h q", q=PD),
                    r_sc2[:, c * 128:(c + 1) * 128].unsqueeze(2)
                    .broadcast_to((128, 128, PD)), ALU.mult),
                    incs=[("c2_p", 1)])
                self.op("scalar", lambda e, c=c: e.activation(
                    sdec[:, :], r_ncum[:, c * 128:(c + 1) * 128], AF.Exp,
                    scale=-1.0), incs=[("c2_sdec", 1)],
                    waits=[("c2_ta", v("c2_ta"))])
                self.op("scalar", lambda e: e.activation(
                    s_bf[:, :], s_sb[:, :], AF.Copy), incs=[("c2_sbf", 1)],
                    waits=[("c2_sup", v("c2_sup"))])
                for g in range(G):
                    self.op("tensor", lambda e, c=c, g=g: e.matmul(
                        ps_g2[:, :], b_ch[c % 2][:, g * 128:(g + 1) * 128],
                        c_ch[c % 2][:, g * 128:(g + 1) * 128],
                        start=True, stop=True), incs=[("c2_g2", 1)],
                        waits=[("cl_bc", 32 * (c + 1)),
                               ("c2_g2e", v("c2_g2e"))])
                    self.op("scalar", lambda e, g=g: e.activation(
                        g2sb[g % 2][:, :], ps_g2[:, :], AF.Copy),
                        incs=[("c2_g2e", 1)],
                        waits=[("c2_g2", v("c2_g2")),
                               ("c2_mt", max(0, v("c2_mt") - 1))])
                    for half in range(2):
                        h0 = (g * 2 + half) * 8
                        pb = kblk % 2
                        self.op("sync", lambda e, c=c, h0=h0, pb=pb: e.dma_start(
                            out=cfl[pb][0:1, :],
                            in_=self.cum_dr[c][h0:h0 + 8, :]
                            .rearrange("h t -> (h t)").unsqueeze(0)),
                            incs=[("cl_cf", 16)],
                            waits=[("c2_bc", max(0, v("c2_bc") - 2)),
                                   ("c2_bdt", max(0, v("c2_bdt") - 1))])
                        self.op("tensor", lambda e, pb=pb: e.matmul(
                            ps_bc[:, 0:8], self.c_ones_f[0:1, :],
                            cfl[pb][0:1, 127:1024:128], start=True, stop=True),
                            incs=[("c2_t", 1)],
                            waits=[("cl_cf", v("cl_cf")),
                                   ("c2_exp", v("c2_exp")),
                                   ("c2_drep", 1)])
                        bw = [("c2_t", v("c2_t"))]
                        if kblk >= 16:
                            bw.append(("c2_sup", v("c2_sup") -
                                       (1 if (kblk % 16) >= 14 else 2)))
                        self.op("scalar", lambda e, h0=h0: e.activation(
                            b_dtot[:, h0 // 8 * 8: h0 // 8 * 8 + 8],
                            ps_bc[:, 0:8], AF.Exp),
                            incs=[("c2_bdt", 1)], waits=bw)
                        for q in range(2):
                            self.op("tensor", lambda e, q=q, pb=pb: e.matmul(
                                ps_bc[:, :], self.c_ones_f[0:1, :],
                                cfl[pb][0:1, q * 512:(q + 1) * 512],
                                start=True, stop=False),
                                waits=[("c2_exp", v("c2_exp")),
                                       ("c2_bdt", v("c2_bdt"))])
                            self.op("tensor", lambda e: e.matmul(
                                ps_bc[:, :], self.c_ident_bf[:, :],
                                self.c_ml512[:, :], start=False, stop=True),
                                incs=[("c2_bc", 1)])
                            for jj in range(4):
                                h = h0 + q * 4 + jj
                                ew = [("c2_bc", v("c2_bc"))]
                                if q == 0 and jj == 0:
                                    ew.append(("c2_mt", max(0, v("c2_mt") - 1)))
                                self.op("scalar", lambda e, c=c, jj=jj, h=h,
                                        pb=pb, q=q: e.activation(
                                    e_all[pb][:, (q * 4 + jj) * 128:
                                              (q * 4 + jj + 1) * 128],
                                    ps_bc[:, jj * 128:(jj + 1) * 128], AF.Exp,
                                    bias=r_ncum[:, c * 128 + h:c * 128 + h + 1]),
                                    incs=[("c2_exp", 1)], waits=ew)
                        self.op("vector", lambda e, g=g, pb=pb: e.tensor_tensor(
                            mt[pb][:, :].rearrange("p (h l) -> p h l", h=8),
                            e_all[pb][:, :].rearrange("p (h l) -> p h l", h=8),
                            g2sb[g % 2][:, :].unsqueeze(1)
                            .broadcast_to((128, 8, 128)), ALU.mult),
                            incs=[("c2_mt", 1)],
                            waits=[("c2_exp", v("c2_exp")),
                                   ("c2_g2e", v("c2_g2e")),
                                   ("c2_ymm", max(0, v("c2_ymm") - 1))])
                        if kblk >= 1:
                            emit_tb(kblk - 1)
                        if kblk >= 2:
                            emit_yf_stt(kblk - 2)
                        self.op("gpsimd", lambda e, c=c, h0=h0, k3=kblk % 3:
                                e.tensor_tensor(
                                    xd_blk[k3][:, :].rearrange(
                                        "p (h q) -> p h q", q=PD),
                                    x_tok[c % 2][:, h0 * PD:(h0 + 8) * PD]
                                    .rearrange("p (h q) -> p h q", q=PD),
                                    d_rep[:, h0:h0 + 8].unsqueeze(2)
                                    .broadcast_to((128, 8, PD)), ALU.mult),
                                incs=[("c2_xd", 1)],
                                waits=[("c2_p", v("c2_p")),
                                       ("c2_drep", 1),
                                       ("c2_yev", max(0, v("c2_yev") - 1))])
                        self.wait("tensor", "c2_mt", v("c2_mt"))
                        self.wait("tensor", "c2_p", v("c2_p"))
                        self.wait("tensor", "c2_sbf", v("c2_sbf"))
                        if kblk >= 2:
                            self.wait("tensor", "c2_tb", v("c2_tb"))
                            self.wait("tensor", "c2_ta", v("c2_ta") - 1)
                            self.wait("tensor", "c2_sup", v("c2_sup"))
                        for j in range(8):
                            h = h0 + j
                            self.op("tensor", lambda e, pb=pb, j=j, h=h: e.matmul(
                                ps_yd[pb][:, j * PD:(j + 1) * PD],
                                mt[pb][:, j * 128:(j + 1) * 128],
                                x_dt[:, h * PD:(h + 1) * PD],
                                start=(j == 0), stop=False,
                                skip_group_check=True))
                        for j in range(8):
                            h = h0 + j
                            self.op("tensor", lambda e, pb=pb, j=j, h=h, g=g,
                                    c=c: e.matmul(
                                ps_yo[pb][:, j * PD:(j + 1) * PD],
                                c_ch[c % 2][:, g * 128:(g + 1) * 128],
                                s_bf[:, h * PD:(h + 1) * PD],
                                start=(j == 0), stop=(j == 7),
                                skip_group_check=True))
                        for j in range(8):
                            h = h0 + j
                            self.op("tensor", lambda e, pb=pb, j=j, h=h, g=g,
                                    c=c: e.matmul(
                                ps_st[pb][:, j * PD:(j + 1) * PD],
                                b_tok[c % 2][:, g * 128:(g + 1) * 128],
                                xd2[:, h * PD:(h + 1) * PD],
                                start=(j == 0), stop=(j == 7),
                                skip_group_check=True),
                                incs=([("c2_ymm", 1)] if j == 7 else []))
                        emit_ta(kblk)
                        kblk += 1
                emit_tb(kblk - 1)
                emit_yf_stt(kblk - 2)
                emit_yf_stt(kblk - 1)
                self.op("scalar", lambda e: e.activation(
                    x_dt[:, :], yf[:, :], AF.Square, accum_out=rcol[:, 0:1]),
                    incs=[("c2_sq", 1)],
                    waits=[("c2_yev", v("c2_yev")),
                           ("c2_spl", max(0, v("c2_spl")))])
                self.op("scalar", lambda e: e.activation(
                    rcol[:, 1:2], rcol[:, 0:1], AF.Sqrt,
                    bias=self.c_misc[:, 0:1], scale=1.0 / INTER),
                    incs=[("c2_sq", 1)])
                self.op("vector", lambda e: e.reciprocal(
                    rcol[:, 2:3], rcol[:, 1:2]), incs=[("c2_rc", 1)],
                    waits=[("c2_sq", v("c2_sq"))])
                self.op("vector", lambda e: e.tensor_scalar(
                    yf[:, :], yf[:, :], rcol[:, 2:3], None, ALU.mult),
                    incs=[("c2_v", 1)], waits=[("c2_rc", v("c2_rc"))])
                self.op("vector", lambda e: e.tensor_tensor(
                    yf[:, :], yf[:, :], gate_t[:, :], ALU.mult),
                    incs=[("c2_yg", 1)], waits=[("cl_g", 16 * (c + 1))])
                self.op("sync", lambda e, c=c: e.dma_start(
                    out=self.yg_dr[c * 128:(c + 1) * 128, :], in_=yf[:, :]),
                    incs=[("c2_spl", 16)], waits=[("c2_yg", v("c2_yg"))])
                self.op("sync", lambda e: e.nop(), incs=[("c2_done", 1)],
                        waits=[("c2_spl", v("c2_spl"))])
                self.op("gpsimd", lambda e: e.nop(), incs=[("c2_done2", 1)],
                        waits=[("c2_yg", v("c2_yg"))])
            self.wait("sync", "c2_spl", v("c2_spl"))


    def phase_D(self, stD):
        v = self.val
        yg_sb = self.sb(stD, "yg_sb", (128, XCH * 1024), BF16)
        w_sb = [self.sb(stD, f"dw{i}", (128, 512), BF16) for i in range(2)]
        att = [self.sb(stD, f"att{i}", (128, 512), F32) for i in range(2)]
        a2 = self.sb(stD, "a2", (128, DIM), F32)
        xo = self.sb(stD, "xo", (128, DIM), F32)
        x2t = self.sb(stD, "x2t", (128, DIM), F32)
        xnt = self.sb(stD, "xnt", (128, DIM), BF16)
        lcol = self.sb(stD, "lcol", (128, 8), F32)
        lscr = a2
        psD = self.psum(stD, "psD", (128, 4096))

        for tg in range(2):
            for kc in range(XCH):
                w = [("d_sp", v("d_sp"))] if kc == 0 and tg > 0 else []
                self.op("sync", lambda e, kc=kc, tg=tg: e.dma_start_transpose(
                    yg_sb[:, kc * 1024:(kc + 1) * 1024],
                    self.yg_dr[tg * 1024:(tg + 1) * 1024,
                               kc * 128:(kc + 1) * 128]),
                    incs=[("d_yg", 16)], waits=w)
            self.wait("tensor", "d_yg", v("d_yg"))
            for do in range(8):
                for kc in range(XCH):
                    self.op("gpsimd", lambda e, tg=tg, do=do, kc=kc: e.dma_start(
                        out=w_sb[kc % 2][:, :], in_=self.w_out_b[do, kc]),
                        incs=[("d_w", 16)],
                        waits=[("d_mm", max(0, v("d_mm") - 1))])
                    for tt in range(8):
                        if kc == 0:
                            self.wait("tensor", "d_sp",
                                      max(0, v("d_sp") - 16 * (7 - tt)))
                        self.op("tensor", lambda e, kc=kc, tt=tt: e.matmul(
                            psD[:, tt * 512:(tt + 1) * 512],
                            yg_sb[:, kc * 1024 + tt * 128:
                                  kc * 1024 + (tt + 1) * 128],
                            w_sb[kc % 2][:, :],
                            start=(kc == 0), stop=(kc == XCH - 1)),
                            incs=([("d_mm", 1)] if tt == 7 else []),
                            waits=([("d_w", 16 * (tg * 512 + do * 64 + kc + 1))]
                                   if tt == 0 else []))
                for tt in range(8):
                    self.op("scalar", lambda e, tt=tt: e.activation(
                        att[tt % 2][:, :], psD[:, tt * 512:(tt + 1) * 512],
                        AF.Copy), incs=[("d_ac", 1)],
                        waits=[("d_mm", v("d_mm")),
                               ("d_sp", max(0, v("d_sp") - 1))])
                    self.op("sync", lambda e, tg=tg, do=do, tt=tt: e.dma_start(
                        out=self.x2_dr[(tg * 8 + tt) * 128:
                                       (tg * 8 + tt + 1) * 128,
                                       do * 512:(do + 1) * 512],
                        in_=att[tt % 2][:, :]),
                        incs=[("d_sp", 16)],
                        waits=[("d_ac", v("d_ac"))])
            self.wait("sync", "d_sp", v("d_sp"))
            self.op("sync", lambda e: e.nop(), incs=[("d_sp2", 1)])
            for tt in range(8):
                tc = tg * 8 + tt
                self.op("sync", lambda e, tc=tc: e.dma_start(
                    out=a2[:, :], in_=self.x2_dr[tc * 128:(tc + 1) * 128, :]),
                    incs=[("l_lda", 16)],
                    waits=[("l_done", v("l_done"))])
                self.op("gpsimd", lambda e, tc=tc: e.dma_start(
                    out=xo[:, :], in_=self.x_in[tc * 128:(tc + 1) * 128, :]),
                    incs=[("l_ldx", 16)],
                    waits=[("l_x2", v("l_x2")), ("d_sp2", v("d_sp2"))])
                self.op("vector", lambda e: e.tensor_add(
                    x2t[:, :], a2[:, :], xo[:, :]), incs=[("l_x2", 1)],
                    waits=[("l_lda", v("l_lda")), ("l_ldx", v("l_ldx"))])
                self.op("scalar", lambda e: e.activation(
                    lscr[:, :], x2t[:, :], AF.Copy, accum_out=lcol[:, 0:1]),
                    incs=[("l_s1", 1)], waits=[("l_x2", v("l_x2"))])
                self.op("vector", lambda e: e.tensor_scalar(
                    lcol[:, 0:1], lcol[:, 0:1], -1.0 / DIM, None, ALU.mult),
                    incs=[("l_nm", 1)], waits=[("l_s1", v("l_s1"))])
                self.op("scalar", lambda e: e.activation(
                    lscr[:, :], x2t[:, :], AF.Square, bias=lcol[:, 0:1],
                    scale=1.0, accum_out=lcol[:, 1:2]),
                    waits=[("l_nm", v("l_nm"))])
                self.op("scalar", lambda e: e.activation(
                    lcol[:, 2:3], lcol[:, 1:2], AF.Ln,
                    bias=self.c_misc[:, 0:1], scale=1.0 / DIM))
                self.op("scalar", lambda e: e.activation(
                    lcol[:, 3:4], lcol[:, 2:3], AF.Exp, scale=-0.5),
                    incs=[("l_rs", 1)])
                self.op("vector", lambda e: e.tensor_scalar(
                    xnt[:, :], x2t[:, :], lcol[:, 0:1], lcol[:, 3:4],
                    ALU.add, ALU.mult), incs=[("l_xn", 1)],
                    waits=[("l_rs", v("l_rs"))])
                self.op("sync", lambda e, tc=tc: e.dma_start(
                    out=self.x2_dr[tc * 128:(tc + 1) * 128, :], in_=x2t[:, :]),
                    incs=[("l_sp", 16)], waits=[("l_xn", v("l_xn"))])
                self.op("sync", lambda e, tc=tc: e.dma_start(
                    out=self.xn2_dr[tc * 128:(tc + 1) * 128, :], in_=xnt[:, :]),
                    incs=[("l_sp", 16)])
                self.op("sync", lambda e: e.nop(), incs=[("l_done", 1)],
                        waits=[("l_sp", v("l_sp"))])

    def phase_E(self, stE):
        v = self.val
        xn2_fm = self.sb(stE, "xn2_fm", (128, NKC * 1024), BF16)
        rt_sb = self.sb(stE, "rt_sb", (128, NKC * 16), BF16)
        at_sb = self.sb(stE, "at_sb", (128, NKC * 64), BF16)
        bias_sb = self.sb(stE, "bias_sb", (1, DIM), BF16)
        w_sb = [self.sb(stE, f"ew{i}", (128, 512), BF16) for i in range(2)]
        bp_sb = [self.sb(stE, f"ebp{i}", (128, 512), BF16) for i in range(2)]
        rheT = self.sb(stE, "rheT", (128, 64 * 128), BF16)
        lg_all = self.sb(stE, "lg_all", (128, 128), F32)
        rh_sb = self.sb(stE, "rh_sb", (128, 8 * 64), F32)
        sm = [self.sb(stE, f"sm{i}", (128, 128), F32) for i in range(3)]
        sc8 = [self.sb(stE, f"sc8{i}", (128, 8), F32) for i in range(4)]
        rhe_t = self.sb(stE, "rhe_t", (128, 1024), BF16)
        x2s = [self.sb(stE, f"x2s{i}", (128, 512), F32) for i in range(2)]
        outs = [self.sb(stE, f"outs{i}", (128, 512), F32) for i in range(2)]
        psE = self.psum(stE, "psE", (128, 4096))

        self.op("gpsimd", lambda e: e.dma_start(
            out=rt_sb[:, :].rearrange("p (k e) -> p k e", k=NKC),
            in_=self.r_t.rearrange("k p e -> p k e")),
            incs=[("e_c", 16)])
        self.op("gpsimd", lambda e: e.dma_start(
            out=at_sb[:, :].rearrange("p (k r) -> p k r", k=NKC),
            in_=self.a_t.rearrange("k p r -> p k r")),
            incs=[("e_c", 16)])
        self.op("gpsimd", lambda e: e.dma_start(
            out=bias_sb[:, :], in_=self.b_base_r), incs=[("e_c", 16)])

        def seg_exp(col8):
            return col8[:, :].unsqueeze(2).broadcast_to((128, 8, 16))

        for tg in range(2):
            for kc in range(NKC):
                w = [("e_mm", max(0, v("e_mm"))) ] if kc == 0 and tg > 0 else []
                self.op("sync", lambda e, kc=kc, tg=tg: e.dma_start_transpose(
                    xn2_fm[:, kc * 1024:(kc + 1) * 1024],
                    self.xn2_dr[tg * 1024:(tg + 1) * 1024,
                                kc * 128:(kc + 1) * 128]),
                    incs=[("e_xn", 16)], waits=w)
            self.wait("tensor", "e_xn", v("e_xn"))
            self.wait("tensor", "e_c", 48)
            for tt in range(8):
                for kc in range(NKC):
                    self.op("tensor", lambda e, kc=kc, tt=tt: e.matmul(
                        psE[:, 0:16],
                        xn2_fm[:, kc * 1024 + tt * 128: kc * 1024 + (tt + 1) * 128],
                        rt_sb[:, kc * 16:(kc + 1) * 16],
                        start=(kc == 0), stop=(kc == NKC - 1),
                        skip_group_check=True))
                for kc in range(NKC):
                    self.op("tensor", lambda e, kc=kc, tt=tt: e.matmul(
                        psE[:, 16:80],
                        xn2_fm[:, kc * 1024 + tt * 128: kc * 1024 + (tt + 1) * 128],
                        at_sb[:, kc * 64:(kc + 1) * 64],
                        start=False, stop=(kc == NKC - 1),
                        skip_group_check=True),
                        incs=([("e_pr", 1)] if kc == NKC - 1 else []))
                self.op("scalar", lambda e, tt=tt: e.activation(
                    lg_all[:, tt * 16:(tt + 1) * 16], psE[:, 0:16], AF.Copy),
                    waits=[("e_pr", v("e_pr"))])
                self.op("scalar", lambda e, tt=tt: e.activation(
                    rh_sb[:, tt * 64:(tt + 1) * 64], psE[:, 16:80], AF.Copy),
                    incs=[("e_pe", 1)])
                self.wait("tensor", "e_pe", v("e_pe"))
            l3 = lambda t: t[:, :].rearrange("p (s e) -> p s e", s=8)
            self.wait("vector", "e_pe", v("e_pe"))
            self.op("vector", lambda e: e.tensor_reduce(
                sc8[0][:, :], l3(lg_all), AX.X, ALU.max), incs=[("bn_vector", 1)])
            self.bounce("vector")
            self.op("vector", lambda e: e.tensor_tensor(
                l3(sm[0]), l3(lg_all), seg_exp(sc8[0]), ALU.subtract),
                incs=[("e_sm", 1)])
            self.op("scalar", lambda e: e.activation(
                sm[1][:, :], sm[0][:, :], AF.Exp), incs=[("e_sme", 1)],
                waits=[("e_sm", v("e_sm"))])
            self.op("vector", lambda e: e.tensor_reduce(
                sc8[1][:, :], l3(sm[1]), AX.X, ALU.add),
                incs=[("bn_vector", 1)], waits=[("e_sme", v("e_sme"))])
            self.bounce("vector")
            self.op("vector", lambda e: e.reciprocal(
                sc8[2][:, :], sc8[1][:, :]), incs=[("bn_vector", 1)])
            self.bounce("vector")
            self.op("vector", lambda e: e.tensor_tensor(
                l3(sm[0]), l3(sm[1]), seg_exp(sc8[2]), ALU.mult),
                incs=[("bn_vector", 1)])
            self.bounce("vector")
            self.op("vector", lambda e: e.tensor_reduce(
                sc8[0][:, :], l3(sm[0]), AX.X, ALU.max), incs=[("bn_vector", 1)])
            self.bounce("vector")
            self.op("vector", lambda e: e.tensor_tensor(
                l3(sm[2]), l3(sm[0]), seg_exp(sc8[0]), ALU.is_ge),
                incs=[("bn_vector", 1)])
            self.bounce("vector")
            self.op("vector", lambda e: e.scalar_tensor_tensor(
                sm[1][:, :], sm[2][:, :], -1e30, sm[0][:, :], ALU.mult,
                ALU.add), incs=[("bn_vector", 1)])
            self.bounce("vector")
            self.op("vector", lambda e: e.tensor_reduce(
                sc8[1][:, :], l3(sm[1]), AX.X, ALU.max), incs=[("bn_vector", 1)])
            self.bounce("vector")
            self.op("vector", lambda e: e.tensor_tensor(
                l3(sm[2]), l3(sm[0]), seg_exp(sc8[1]), ALU.is_ge),
                incs=[("bn_vector", 1)])
            self.bounce("vector")
            self.op("vector", lambda e: e.tensor_tensor(
                sm[1][:, :], sm[0][:, :], sm[2][:, :], ALU.mult),
                incs=[("bn_vector", 1)])
            self.bounce("vector")
            self.op("vector", lambda e: e.tensor_reduce(
                sc8[3][:, :], l3(sm[1]), AX.X, ALU.add), incs=[("bn_vector", 1)])
            self.bounce("vector")
            self.op("vector", lambda e: e.tensor_scalar(
                sc8[3][:, :], sc8[3][:, :], 1e-6, None, ALU.add),
                incs=[("bn_vector", 1)])
            self.bounce("vector")
            self.op("vector", lambda e: e.reciprocal(
                sc8[2][:, :], sc8[3][:, :]), incs=[("bn_vector", 1)])
            self.bounce("vector")
            self.op("vector", lambda e: e.tensor_scalar(
                sc8[2][:, :], sc8[2][:, :], 0.5, None, ALU.mult),
                incs=[("bn_vector", 1)])
            self.bounce("vector")
            self.op("vector", lambda e: e.tensor_tensor(
                l3(sm[0]), l3(sm[1]), seg_exp(sc8[2]), ALU.mult),
                incs=[("e_cmb", 1)])
            if self.debug:
                self.op("sync", lambda e, tg=tg: e.dma_start(
                    out=self.edbg_dr[tg][:, 0:128], in_=lg_all[:, :]),
                    incs=[("e_dbg", 16)], waits=[("e_cmb", v("e_cmb"))])
                self.op("sync", lambda e, tg=tg: e.dma_start(
                    out=self.edbg_dr[tg][:, 128:256], in_=sm[0][:, :]),
                    incs=[("e_dbg", 16)])
                self.wait("sync", "e_dbg", v("e_dbg"))
            for tt in range(8):
                self.op("vector", lambda e, tt=tt: e.tensor_tensor(
                    rhe_t[:, :].rearrange("p (x r) -> p x r", x=16),
                    rh_sb[:, tt * 64:(tt + 1) * 64].unsqueeze(1)
                    .broadcast_to((128, 16, PD)),
                    sm[0][:, tt * 16:(tt + 1) * 16].unsqueeze(2)
                    .broadcast_to((128, 16, PD)), ALU.mult),
                    incs=[("e_rhe", 1)],
                    waits=([("e_cmb", v("e_cmb"))] if tt == 0 else
                           [("e_rsp", v("e_rsp"))]))
                self.op("sync", lambda e, tt=tt, tg=tg: e.dma_start(
                    out=self.rhe_dr[tg * 8 + tt], in_=rhe_t[:, :]),
                    incs=[("e_rsp", 16)], waits=[("e_rhe", v("e_rhe"))])
                self.wait("sync", "e_rsp", v("e_rsp"))
                for pair in range(8):
                    self.op("sync", lambda e, tt=tt, tg=tg, pair=pair:
                            e.dma_start_transpose(
                                rheT[:, (tt * 8 + pair) * 128:
                                     (tt * 8 + pair + 1) * 128],
                                self.rhe_dr[tg * 8 + tt][:, pair * 128:
                                                         (pair + 1) * 128]),
                            incs=[("e_rt", 16)])
            self.wait("tensor", "e_rt", v("e_rt"))
            for do in range(8):
                for kc in range(NKC):
                    self.op("gpsimd", lambda e, do=do, kc=kc: e.dma_start(
                        out=w_sb[kc % 2][:, :], in_=self.w_base_b[do, kc]),
                        incs=[("e_w", 16)],
                        waits=[("e_mm", max(0, v("e_mm") - 1))])
                    for tt in range(8):
                        if kc == 0:
                            self.wait("tensor", "e_ev",
                                      max(0, v("e_ev") - (7 - tt)))
                        self.op("tensor", lambda e, kc=kc, tt=tt: e.matmul(
                            psE[:, tt * 512:(tt + 1) * 512],
                            xn2_fm[:, kc * 1024 + tt * 128:
                                   kc * 1024 + (tt + 1) * 128],
                            w_sb[kc % 2][:, :],
                            start=(kc == 0), stop=False,
                            skip_group_check=True),
                            incs=([("e_mm", 1)] if tt == 7 else []),
                            waits=([("e_w", 16 * v("e_w") // 16)]
                                   if tt == 0 else []))
                for tt in range(8):
                    self.op("tensor", lambda e, tt=tt, do=do: e.matmul(
                        psE[:, tt * 512:(tt + 1) * 512],
                        self.c_ones_bf[0:1, :],
                        bias_sb[0:1, do * 512:(do + 1) * 512],
                        start=False, stop=False, skip_group_check=True))
                for pair in range(8):
                    self.op("gpsimd", lambda e, pair=pair, do=do: e.dma_start(
                        out=bp_sb[pair % 2][:, :], in_=self.b_pair[pair, do]),
                        incs=[("e_bw", 16)],
                        waits=[("e_mm2", max(0, v("e_mm2") - 1))])
                    for tt in range(8):
                        self.op("tensor", lambda e, pair=pair, tt=tt: e.matmul(
                            psE[:, tt * 512:(tt + 1) * 512],
                            rheT[:, (tt * 8 + pair) * 128:
                                 (tt * 8 + pair + 1) * 128],
                            bp_sb[pair % 2][:, :],
                            start=False, stop=(pair == 7),
                            skip_group_check=True),
                            incs=([("e_mm2", 1)] if tt == 7 else []),
                            waits=([("e_bw", 16 * v("e_bw") // 16)]
                                   if tt == 0 else []))
                for tt in range(8):
                    tc = tg * 8 + tt
                    self.op("sync", lambda e, tc=tc, do=do, tt=tt: e.dma_start(
                        out=x2s[tt % 2][:, :],
                        in_=self.x2_dr[tc * 128:(tc + 1) * 128,
                                       do * 512:(do + 1) * 512]),
                        incs=[("e_x2", 16)],
                        waits=[("e_fv", max(0, v("e_fv") - 1))])
                    self.op("vector", lambda e, tt=tt: e.tensor_tensor(
                        outs[tt % 2][:, :], x2s[tt % 2][:, :],
                        psE[:, tt * 512:(tt + 1) * 512], ALU.add),
                        incs=[("e_fv", 1), ("e_ev", 1)] if False else
                             [("e_fv", 1)],
                        waits=[("e_x2", v("e_x2")), ("e_mm2", v("e_mm2"))])
                    self.op("sync", lambda e, tc=tc, do=do, tt=tt: e.dma_start(
                        out=self.out_dr[tc * 128:(tc + 1) * 128,
                                        do * 512:(do + 1) * 512],
                        in_=outs[tt % 2][:, :]),
                        incs=[("e_out", 16), ("e_ev", 1)] if False else
                             [("e_out", 16)],
                        waits=[("e_fv", v("e_fv"))])
                    self.op("sync", lambda e: e.nop(), incs=[("e_ev", 1)],
                            waits=[("e_out", v("e_out"))])
        self.wait("sync", "e_out", v("e_out"))
        self.wait("sync", "e_rsp", v("e_rsp"))

    def install(self):
        nc = self.nc
        with nc.Block() as block:
            @block.tensor
            def _(e):
                for f in self.L["tensor"]:
                    f(e)

            @block.vector
            def _(e):
                for f in self.L["vector"]:
                    f(e)

            @block.scalar
            def _(e):
                for f in self.L["scalar"]:
                    f(e)

            @block.gpsimd
            def _(e):
                for f in self.L["gpsimd"]:
                    f(e)

            @block.sync
            def _(e):
                for f in self.L["sync"]:
                    f(e)
        self.stack.close()



MID_NAMES = ["xch_dr", "bch_dr", "cch_dr", "gate_dr", "dtk_dr", "sc2_dr",
             "ncum_dr", "cum_dr"]


def _run(inputs, debug=False):
    in_maps = _host_prep(inputs)
    mk1 = MK(debug=debug, split=1)
    nc1 = mk1.build()
    res1 = run_bass_kernel_spmd(nc1, in_maps, core_ids=list(range(8)))
    in_maps2 = []
    for core in range(8):
        m = dict(in_maps[core])
        r = res1.results[core]
        for n in MID_NAMES:
            m[n] = np.asarray(r[n])
        m["s_peer"] = np.asarray(res1.results[core ^ 1]["sfin_dr"])
        in_maps2.append(m)
    mk2 = MK(debug=debug, split=2)
    nc2 = mk2.build()
    res2 = run_bass_kernel_spmd(nc2, in_maps2, core_ids=list(range(8)))
    return res1, res2


def kernel(**inputs):
    _, res2 = _run(inputs)
    full = np.zeros((4, 4096, DIM), np.float32)
    for core in range(8):
        b, hh = divmod(core, 2)
        full[b, hh * T:(hh + 1) * T] = np.asarray(
            res2.results[core]["out"], dtype=np.float32)
    return full
